# version 04 (final)
# baseline (speedup 1.0000x reference)
import contextlib
import numpy as np
import concourse.bass as bass
import concourse.mybir as mybir
from concourse.alu_op_type import AluOpType
from concourse.bass_utils import run_bass_kernel_spmd

F32 = mybir.dt.float32
BF16 = mybir.dt.bfloat16
AF = mybir.ActivationFunctionType
AX = mybir.AxisListType
OP = AluOpType

B, N, C = 16, 4096, 1024
NCORES = 8
BPC = B // NCORES
TT = 512
NTT = N // TT
NCH = C // 128
HEADS, DH = 16, 64
FFN0 = 4096
EXP_HID, SH_HID = 2816, 1536
NE = 8
EPS = 1e-5
ATT_SCALE1 = float(DH ** -0.25)
AGENT_SCALE = float(C ** (-0.5))


class Tok:
    __slots__ = ("sem", "val")
    def __init__(self, sem, val):
        self.sem = sem
        self.val = val


class Sem:
    def __init__(self, h, name):
        self.h = h
        self.name = name
        self.n = 0

    @property
    def is_dma(self):
        return self.name.startswith("sD_") or self.name in ("sSP", "sGD")

    def inc(self, inst, k=1):
        inst.then_inc(self.h, k)
        self.n += k
        return Tok(self, self.n)


def _merge(toks):
    best = {}
    for t in toks:
        if t is None:
            continue
        if t.sem.name not in best or best[t.sem.name].val < t.val:
            best[t.sem.name] = t
    return list(best.values())


def wait(eng, *toks):
    for t in _merge(toks):
        eng.wait_ge(t.sem.h, t.val)


class Buf:
    __slots__ = ("ap", "wr", "rd")
    def __init__(self, ap=None):
        self.ap = ap
        self.wr = []
        self.rd = []

    def pre_write(self):
        return _merge(self.rd + self.wr)

    def set_write(self, tok):
        self.wr = [tok]
        self.rd = []

    def pre_read(self):
        return _merge(self.wr)

    def add_read(self, tok):
        self.rd.append(tok)
        if len(self.rd) > 8:
            self.rd = _merge(self.rd)

    def add_write(self, tok):
        self.wr.append(tok)
        if len(self.wr) > 8:
            self.wr = _merge(self.wr)


class Kit:
    def __init__(self, nc, stack):
        self.nc = nc
        self.stack = stack
        self.blk = stack.enter_context(nc.Block())
        mk = lambda name: Sem(stack.enter_context(nc.semaphore(name)), name)
        self.sSP = mk("sSP")
        self.sGD = mk("sGD")
        self.sPE = mk("sPE")
        self.sDV = mk("sDV")
        self.sSC = mk("sSC")
        self.sGC = mk("sGC")
        self.sFN = mk("sFN")
        self._mk = mk
        self._dsems = {}
        self.nfence = 0
        self.bfp = stack.enter_context(nc.sbuf_tensor("poolbf", [128, 78 * 1024], BF16))
        self.fpp = stack.enter_context(nc.sbuf_tensor("poolfp", [128, 12 * 1024], F32))
        self.cpool = stack.enter_context(nc.sbuf_tensor("cpool", [128, 16], F32))
        self.ident = stack.enter_context(nc.sbuf_tensor("ident", [128, 128], BF16))
        self.identf = stack.enter_context(nc.sbuf_tensor("identf", [128, 128], F32))
        self.iotat = stack.enter_context(nc.sbuf_tensor("iotat", [128, 128], F32))
        self.ones_col = stack.enter_context(nc.sbuf_tensor("ones_col", [128, 1], F32))
        self.onesb_col = stack.enter_context(nc.sbuf_tensor("onesb_col", [128, 1], BF16))
        self.onesb_row = stack.enter_context(nc.sbuf_tensor("onesb_row", [1, 128], BF16))
        self.ones_rowf = stack.enter_context(nc.sbuf_tensor("ones_rowf", [1, 128], F32))
        self.gb_cols = stack.enter_context(nc.sbuf_tensor("gb_cols", [128, 16], F32))
        self._consts = {}
        self._cn = 0
        self._init_done = False
        self.init_tok = None
        self._psums = {}
        self._bufs = {}
        self._phase_stack = None
        self.last_fence = None
        self._waited = {}
        self._pending = []
        self._pending_eng = None
        self._bf_ptr = 0
        self._fp_ptr = 0

    def bf(self, lo, hi):
        assert 0 <= lo < hi <= 78 * 1024
        return self.bfp[:, lo:hi]

    def fp(self, lo, hi):
        assert 0 <= lo < hi <= 12 * 1024
        return self.fpp[:, lo:hi]

    def const(self, val):
        val = float(val)
        if val in (0.0, 1.0):
            return val
        if val not in self._consts:
            assert not self._init_done, "declare consts before emit_init"
            self._consts[val] = self._cn
            self.nc.const_aps.register(F32, val, self.cpool[:, self._cn:self._cn + 1])
            self._cn += 1
            assert self._cn <= 16
        return val

    def phase(self, name):
        if self._phase_stack is not None:
            self._phase_stack.close()
            self._psums.clear()
        self.last_fence = self.fence()
        self._phase_stack = contextlib.ExitStack()
        self._bf_ptr = 0
        self._fp_ptr = 0
        self._phase_name = name

    def balloc(self, nelems):
        lo = self._bf_ptr
        self._bf_ptr += nelems
        return self.bf(lo, self._bf_ptr)

    def falloc(self, nelems):
        lo = self._fp_ptr
        self._fp_ptr += nelems
        return self.fp(lo, self._fp_ptr)

    def pbuf(self, name, nelems, dtype=F32):
        key = self._phase_name + ":" + name
        if key not in self._bufs:
            ap = self.falloc(nelems) if dtype == F32 else self.balloc(nelems)
            self._bufs[key] = Buf(ap)
        return self._bufs[key]

    def ppsum(self, name, dtype=F32):
        key = self._phase_name + ":" + name
        if key not in self._bufs:
            shape = [128, 512] if dtype == F32 else [128, 1024]
            t = self._phase_stack.enter_context(
                self.nc.psum_tensor(self._phase_name + "_" + name, shape, dtype))
            self._bufs[key] = Buf(t[:])
        return self._bufs[key]

    def nbuf(self, name, ap=None):
        if name not in self._bufs:
            assert ap is not None, f"buf {name} not yet created"
            self._bufs[name] = Buf(ap)
        return self._bufs[name]

    def dsem(self, name):
        if name not in self._dsems:
            self._dsems[name] = self._mk("sD_" + name)
        return self._dsems[name]

    def psum(self, name, dtype=F32):
        if name not in self._psums:
            shape = [128, 512] if dtype == F32 else [128, 1024]
            t = self.stack.enter_context(self.nc.psum_tensor(name, shape, dtype))
            self._psums[name] = t
        return self._psums[name]

    def flush(self):
        if self._pending:
            bodies = self._pending
            self._pending = []
            sec = getattr(self.blk, self._pending_eng)
            def run_all(eng, bodies=bodies):
                for bdy in bodies:
                    bdy(eng)
            sec(run_all)
        self._pending_eng = None

    def fence(self):
        self.flush()
        self.nfence += 1
        target = 5 * self.nfence
        dma_totals = [(s.h, s.n) for s in
                      [self.sSP, self.sGD] + list(self._dsems.values()) if s.n]
        for i, sec in enumerate((self.blk.tensor, self.blk.vector, self.blk.scalar,
                                 self.blk.gpsimd, self.blk.sync)):
            def body(eng, target=target, first=(i == 4)):
                if first:
                    for h, n in dma_totals:
                        eng.wait_ge(h, n)
                eng.drain()
                eng.nop().then_inc(self.sFN.h, 1)
                eng.wait_ge(self.sFN.h, target)
            sec(body)
        self.sFN.n = target
        return Tok(self.sFN, target)

    def op(self, section, sem, fn, reads=(), writes=(), nowaw=(), extra=()):
        def body(eng):
            toks = []
            for bb in writes:
                toks += bb.pre_write()
            for bb in nowaw:
                toks += _merge(bb.rd)
            for bb in reads:
                toks += bb.pre_read()
            toks += list(extra)
            toks.append(self.last_fence)
            toks = [Tok(t.sem, t.sem.n) if (t is not None and t.sem.is_dma) else t
                    for t in toks]
            seen = self._waited.setdefault(eng.engine.value, {})
            keep = []
            for t in _merge(toks):
                if seen.get(t.sem.name, -1) >= t.val:
                    continue
                seen[t.sem.name] = t.val
                keep.append(t)
            for t in keep:
                eng.wait_ge(t.sem.h, t.val)
            inst = fn(eng)
            k = 16 if (sem in (self.sSP, self.sGD)
                       or sem.name.startswith("sD_")) else 1
            tok = sem.inc(inst, k)
            for bb in writes:
                bb.set_write(tok)
            for bb in nowaw:
                bb.add_write(tok)
            for bb in reads:
                bb.add_read(tok)
        name = section.__name__
        if self._pending_eng != name:
            self.flush()
            self._pending_eng = name
        self._pending.append(body)


def emit_wcast(K, casts):
    K.phase("wcast")
    for i, (srcap, dstap) in enumerate(casts):
        K.op(K.blk.gpsimd, K.dsem(f"wc{i % 4}"),
             lambda g, srcap=srcap, dstap=dstap: g.dma_start(out=dstap, in_=srcap))

def emit_init(K, gamma_dr, beta_dr):
    K._init_done = True
    consts = dict(K._consts)

    def g_body(g):
        for val, i in consts.items():
            g.memset(K.cpool[:, i:i + 1], val)
        t_io = K.sGC.inc(g.iota(K.iotat[:], [[1, 128]], base=0, channel_multiplier=-1,
                                allow_small_or_imprecise_dtypes=True))
        wait(g, t_io)
        K.sGC.inc(g.tensor_single_scalar(K.ident[:], K.iotat[:], 0.0, OP.is_equal))
        wait(g, t_io)
        K.sGC.inc(g.tensor_single_scalar(K.identf[:], K.iotat[:], 0.0, OP.is_equal))
        g.memset(K.cpool[:, 15:16], 0.0)
    K.blk.gpsimd(g_body)

    def v_body(v):
        v.memset(K.ones_col[:], 1.0)
        v.memset(K.onesb_col[:], 1.0)
        v.memset(K.onesb_row[:], 1.0)
        v.memset(K.ones_rowf[:], 1.0)
    K.blk.vector(v_body)

    def s_body(sy):
        for ci in range(NCH):
            sy.dma_start(out=K.gb_cols[:, ci:ci + 1],
                         in_=gamma_dr[ci * 128:(ci + 1) * 128].unsqueeze(1)
                         ).then_inc(K.sSP.h, 16)
            sy.dma_start(out=K.gb_cols[:, 8 + ci:9 + ci],
                         in_=beta_dr[ci * 128:(ci + 1) * 128].unsqueeze(1)
                         ).then_inc(K.sSP.h, 16)
        K.sSP.n += 32 * NCH
        sy.wait_ge(K.sSP.h, K.sSP.n)
    K.blk.sync(s_body)

    K.init_tok = K.fence()
    K.last_fence = K.init_tok


def emit_entry(K, b, x_in, xT_sp, xn_sp, xln_sp, inner_ln=True):
    NTILES = N // 128
    eps = K.const(EPS)
    mhalf = K.const(-0.5)

    K.phase(f"en{b}")
    def pair(name, nel, dtype=F32):
        return [K.pbuf(f"{name}{j}", nel, dtype) for j in range(2)]
    Xt = pair("Xt", 1024)
    XnF = pair("XnF", 1024)
    XpF = pair("XpF", 1024)
    xTs = pair("xTs", 1024)
    st6 = pair("st6", 12)
    st2 = pair("st2", 2)
    st2b = pair("st2b", 2)
    rS = pair("rS", 1)
    r2S = pair("r2S", 1)
    tmp1 = pair("tmp1", 1)
    xn_b = pair("xnb", 1024, BF16)
    xp_b = pair("xpb", 1024, BF16)
    xnTs = pair("xnTs", 1024, BF16)
    xlnTs = pair("xlnTs", 1024, BF16)

    pA = K.ppsum("pA", BF16)
    pB = K.ppsum("pB", BF16)
    pC = K.ppsum("pC")
    pA2 = K.ppsum("pA2", BF16)
    pB2 = K.ppsum("pB2", BF16)
    pC2 = K.ppsum("pC2")

    for i in range(NTILES):
        p = i % 2
        xt, xnf, xpf = Xt[p], XnF[p], XpF[p]
        K.op(K.blk.sync, K.dsem(f"eld{p}"),
             lambda sy, i=i, xt=xt: sy.dma_start(
                 out=xt.ap, in_=x_in[(b * N + i * 128):(b * N + (i + 1) * 128), :]),
             writes=[xt])
        K.op(K.blk.vector, K.sDV,
             lambda v, p=p: v.bn_stats(st6[p].ap[:, 0:6], Xt[p].ap[:, 0:512]),
             reads=[xt], writes=[st6[p]])
        K.op(K.blk.vector, K.sDV,
             lambda v, p=p: v.bn_stats(st6[p].ap[:, 6:12], Xt[p].ap[:, 512:1024]),
             reads=[xt], writes=[st6[p]])
        K.op(K.blk.vector, K.sDV, lambda v, p=p: v.bn_aggr(st2[p].ap, st6[p].ap),
             reads=[st6[p]], writes=[st2[p]])
        K.op(K.blk.scalar, K.sSC,
             lambda sc, p=p: sc.activation(tmp1[p].ap, st2[p].ap[:, 1:2], AF.Ln, bias=eps),
             reads=[st2[p]], writes=[tmp1[p]], extra=[K.init_tok])
        K.op(K.blk.scalar, K.sSC,
             lambda sc, p=p: sc.activation(rS[p].ap, tmp1[p].ap, AF.Exp, scale=mhalf),
             reads=[tmp1[p]], writes=[rS[p]])
        K.op(K.blk.vector, K.sDV,
             lambda v, p=p: v.tensor_scalar(XnF[p].ap, Xt[p].ap, st2[p].ap[:, 0:1],
                                            rS[p].ap, OP.subtract, OP.mult),
             reads=[xt, st2[p], rS[p]], writes=[xnf])
        if inner_ln:
            K.op(K.blk.vector, K.sDV,
                 lambda v, p=p: v.bn_stats(st6[p].ap[:, 0:6], XnF[p].ap[:, 0:512]),
                 reads=[xnf], writes=[st6[p]])
            K.op(K.blk.vector, K.sDV,
                 lambda v, p=p: v.bn_stats(st6[p].ap[:, 6:12], XnF[p].ap[:, 512:1024]),
                 reads=[xnf], writes=[st6[p]])
            K.op(K.blk.vector, K.sDV, lambda v, p=p: v.bn_aggr(st2b[p].ap, st6[p].ap),
                 reads=[st6[p]], writes=[st2b[p]])
            K.op(K.blk.scalar, K.sSC,
                 lambda sc, p=p: sc.activation(tmp1[p].ap, st2b[p].ap[:, 1:2], AF.Ln,
                                               bias=eps),
                 reads=[st2b[p]], writes=[tmp1[p]])
            K.op(K.blk.scalar, K.sSC,
                 lambda sc, p=p: sc.activation(r2S[p].ap, tmp1[p].ap, AF.Exp, scale=mhalf),
                 reads=[tmp1[p]], writes=[r2S[p]])
            K.op(K.blk.vector, K.sDV,
                 lambda v, p=p: v.tensor_scalar(XpF[p].ap, XnF[p].ap, st2b[p].ap[:, 0:1],
                                                r2S[p].ap, OP.subtract, OP.mult),
                 reads=[xnf, st2b[p], r2S[p]], writes=[xpf])
        K.op(K.blk.vector, K.sDV, lambda v, p=p: v.tensor_copy(xn_b[p].ap, XnF[p].ap),
             reads=[xnf], writes=[xn_b[p]])
        if inner_ln:
            K.op(K.blk.vector, K.sDV, lambda v, p=p: v.tensor_copy(xp_b[p].ap, XpF[p].ap),
                 reads=[xpf], writes=[xp_b[p]])
        for ci in range(NCH):
            sl = slice(ci * 128, (ci + 1) * 128)
            qA, qB, qC = (pA, pB, pC) if ci % 2 == 0 else (pA2, pB2, pC2)
            K.op(K.blk.tensor, K.sPE,
                 lambda t, p=p, sl=sl, qA=qA: t.transpose(qA.ap[:, 0:128],
                                                          xn_b[p].ap[:, sl], K.ident[:]),
                 reads=[xn_b[p]], writes=[qA], extra=[K.init_tok])
            K.op(K.blk.vector, K.sDV,
                 lambda v, p=p, sl=sl, qA=qA: v.tensor_copy(xnTs[p].ap[:, sl],
                                                            qA.ap[:, 0:128]),
                 reads=[qA], writes=[xnTs[p]])
            if inner_ln:
                K.op(K.blk.tensor, K.sPE,
                     lambda t, p=p, sl=sl, qB=qB: t.transpose(qB.ap[:, 0:128],
                                                              xp_b[p].ap[:, sl],
                                                              K.ident[:]),
                     reads=[xp_b[p]], writes=[qB])
                K.op(K.blk.vector, K.sDV,
                     lambda v, p=p, sl=sl, ci=ci, qB=qB: v.tensor_scalar(
                         xlnTs[p].ap[:, sl], qB.ap[:, 0:128],
                         K.gb_cols[:, ci:ci + 1], K.gb_cols[:, 8 + ci:9 + ci],
                         OP.mult, OP.add),
                     reads=[qB], writes=[xlnTs[p]])
            K.op(K.blk.tensor, K.sPE,
                 lambda t, p=p, sl=sl, qC=qC: t.transpose(qC.ap[:, 0:128],
                                                          Xt[p].ap[:, sl], K.identf[:]),
                 reads=[xt], writes=[qC])
            K.op(K.blk.vector, K.sDV,
                 lambda v, p=p, sl=sl, qC=qC: v.tensor_copy(xTs[p].ap[:, sl],
                                                            qC.ap[:, 0:128]),
                 reads=[qC], writes=[xTs[p]])
        tsl = slice(i * 128, (i + 1) * 128)
        K.op(K.blk.sync, K.dsem(f"espn{p}"),
             lambda sy, p=p, tsl=tsl: sy.dma_start(
                 out=xn_sp[b].rearrange("(ci p) t -> p ci t", p=128)[:, :, tsl],
                 in_=xnTs[p].ap.rearrange("p (ci t) -> p ci t", ci=NCH)),
             reads=[xnTs[p]])
        if inner_ln:
            K.op(K.blk.sync, K.dsem(f"espl{p}"),
                 lambda sy, p=p, tsl=tsl: sy.dma_start(
                     out=xln_sp[b].rearrange("(ci p) t -> p ci t", p=128)[:, :, tsl],
                     in_=xlnTs[p].ap.rearrange("p (ci t) -> p ci t", ci=NCH)),
                 reads=[xlnTs[p]])
        K.op(K.blk.sync, K.dsem(f"espx{p}"),
             lambda sy, p=p, tsl=tsl: sy.dma_start(
                 out=xT_sp[b].rearrange("(ci p) t -> p ci t", p=128)[:, :, tsl],
                 in_=xTs[p].ap.rearrange("p (ci t) -> p ci t", ci=NCH)),
             reads=[xTs[p]])



def emit_qkv(K, b, xln_sp, wq_dr, wk_dr, wv_dr, q_sp, k_sp, v_sp):
    K.phase(f"qkv{b}")
    s1 = K.const(ATT_SCALE1)
    Wqb = K.pbuf("Wqb", NCH * 1024, BF16)
    Wkb = K.pbuf("Wkb", NCH * 1024, BF16)
    Wvb = K.pbuf("Wvb", NCH * 1024, BF16)
    xt = [K.pbuf(f"xt{j}", NCH * TT, BF16) for j in range(2)]
    qk_st = [K.pbuf(f"qkst{j}", TT, BF16) for j in range(4)]
    v_st = [K.pbuf(f"vst{j}", TT, BF16) for j in range(4)]
    pQ = K.ppsum("pQ")
    pK = K.ppsum("pK")
    pV = [K.ppsum("pV0"), K.ppsum("pV1")]

    for ci in range(NCH):
        rs = slice(ci * 128, (ci + 1) * 128)
        cs = slice(ci * 1024, (ci + 1) * 1024)
        K.op(K.blk.sync, K.dsem("qwk"),
             lambda sy, rs=rs, cs=cs: sy.dma_start(out=Wkb.ap[:, cs], in_=wk_dr[rs, :]),
             nowaw=[Wkb])
        K.op(K.blk.scalar, K.dsem("qwv"),
             lambda sc, rs=rs, cs=cs: sc.dma_start(out=Wvb.ap[:, cs], in_=wv_dr[rs, :]),
             nowaw=[Wvb])
        K.op(K.blk.sync, K.dsem("qwq"),
             lambda sy, rs=rs, cs=cs: sy.dma_start(out=Wqb.ap[:, cs], in_=wq_dr[rs, :]),
             nowaw=[Wqb])

    for tt in range(NTT):
        p = tt % 2
        tsl = slice(tt * TT, (tt + 1) * TT)
        K.op(K.blk.sync, K.dsem(f"qxl{p}"),
             lambda sy, p=p, tsl=tsl: sy.dma_start(
                 out=xt[p].ap.rearrange("p (ci t) -> p ci t", ci=NCH),
                 in_=xln_sp[b].rearrange("(ci p) t -> p ci t", p=128)[:, :, tsl]),
             writes=[xt[p]])
        for jc in range(NCH):
            for ci in range(NCH):
                for (pp, Wb) in ((pQ, Wqb), (pK, Wkb)):
                    K.op(K.blk.tensor, K.sPE,
                         lambda t, pp=pp, Wb=Wb, ci=ci, jc=jc, p=p: t.matmul(
                             pp.ap, Wb.ap[:, ci * 1024 + jc * 128:ci * 1024 + (jc + 1) * 128],
                             xt[p].ap[:, ci * TT:(ci + 1) * TT],
                             start=(ci == 0), stop=(ci == NCH - 1)),
                         reads=[Wb, xt[p]], writes=[pp])
            for g_, (pp, sp) in enumerate(((pQ, q_sp), (pK, k_sp))):
                st = qk_st[2 * (jc % 2) + g_]
                K.op(K.blk.vector, K.sDV,
                     lambda v, pp=pp, st=st: v.tensor_scalar(
                         st.ap, pp.ap, s1, None, OP.mult),
                     reads=[pp], writes=[st])
                K.op(K.blk.sync, K.dsem(f"qksp{2 * (jc % 2) + g_}"),
                     lambda sy, st=st, sp=sp, jc=jc, tsl=tsl: sy.dma_start(
                         out=sp[b, 2 * jc:2 * jc + 2, :, tsl].rearrange(
                             "h d t -> (h d) t"),
                         in_=st.ap),
                     reads=[st])
        for tc in range(4):
            for ci in range(NCH):
                for jh in range(2):
                    pp = pV[jh]
                    K.op(K.blk.tensor, K.sPE,
                         lambda t, pp=pp, ci=ci, tc=tc, jh=jh, p=p: t.matmul(
                             pp.ap,
                             xt[p].ap[:, ci * TT + tc * 128:ci * TT + (tc + 1) * 128],
                             Wvb.ap[:, ci * 1024 + jh * TT:ci * 1024 + (jh + 1) * TT],
                             start=(ci == 0), stop=(ci == NCH - 1)),
                         reads=[Wvb, xt[p]], writes=[pp])
            for jh in range(2):
                pp = pV[jh]
                st = v_st[2 * (tc % 2) + jh]
                K.op(K.blk.vector, K.sDV,
                     lambda v, pp=pp, st=st: v.tensor_copy(st.ap, pp.ap),
                     reads=[pp], writes=[st])
                K.op(K.blk.sync, K.dsem(f"vsp{2 * (tc % 2) + jh}"),
                     lambda sy, st=st, tc=tc, jh=jh, tsl=tsl, tt=tt: sy.dma_start(
                         out=v_sp[b, tt * TT + tc * 128:tt * TT + (tc + 1) * 128,
                                  jh * TT:(jh + 1) * TT],
                         in_=st.ap),
                     reads=[st])



def emit_attn(K, b, q_sp, k_sp, v_sp, wo_dr, o_sp, attn_sp):
    K.phase(f"att{b}")
    NKC = N // 128
    mh2 = K.const(-0.5)

    Kbuf = [K.pbuf(f"Kb{j}", 4096, BF16) for j in range(2)]
    KbOne = [K.pbuf(f"Kb1_{j}", 0, BF16) for j in range(0)]
    Vbuf = [K.pbuf(f"Vb{j}", NKC * 65, BF16) for j in range(2)]
    Qbuf = [K.pbuf(f"Qb{j}", TT, BF16) for j in range(2)]
    ksqb = K.pbuf("ksqb", 4096, BF16)
    qsqb = K.pbuf("qsqb", TT, BF16)
    Sbuf = K.pbuf("Sbuf", NKC * TT, BF16)
    Ost = [K.pbuf(f"Ost{j}", TT, BF16) for j in range(2)]
    recb = K.pbuf("recb", TT, BF16)
    rowb = K.pbuf("rowb", TT)
    qnrow = K.pbuf("qnrow", TT)
    brow = K.pbuf("brow", TT)
    recrow = K.pbuf("recrow", TT)
    recB = K.pbuf("recB", TT)
    kmaxs = K.pbuf("kmaxs", 16)
    konesA = [K.nbuf(f"at_konesA{b}_{j}", Kbuf[j].ap[64:65, :]) for j in range(2)]
    vonesA = [K.nbuf(f"at_vonesA{b}_{j}",
                     Vbuf[j].ap.rearrange("p (kc u) -> p kc u", u=65)[:, :, 64:65])
              for j in range(2)]

    pS = [K.ppsum("pS0"), K.ppsum("pS1")]
    pO = K.ppsum("pO")
    pN = K.ppsum("pN")
    pR = K.ppsum("pR")

    for j in range(2):
        K.op(K.blk.vector, K.sDV, lambda v, j=j: v.memset(konesA[j].ap, 1.0),
             writes=[konesA[j]])
        K.op(K.blk.vector, K.sDV, lambda v, j=j: v.memset(vonesA[j].ap, 1.0),
             writes=[vonesA[j]])

    for h in range(HEADS):
        j = h % 2
        kb, vb = Kbuf[j], Vbuf[j]
        K.op(K.blk.sync, K.dsem(f"akl{j}"),
             lambda sy, h=h, kb=kb: sy.dma_start(out=kb.ap[0:64, :], in_=k_sp[b, h]),
             writes=[kb])
        K.op(K.blk.sync, K.dsem(f"avl{j}"),
             lambda sy, h=h, vb=vb: sy.dma_start(
                 out=vb.ap.rearrange("p (kc u) -> p kc u", u=65)[:, :, 0:64],
                 in_=v_sp[b, :, h * DH:(h + 1) * DH].rearrange(
                     "(kc p) d -> p kc d", p=128)),
             writes=[vb])
        K.op(K.blk.scalar, K.sSC,
             lambda sc, kb=kb: sc.activation(ksqb.ap[0:64, :], kb.ap[0:64, :],
                                             AF.Square),
             reads=[kb], writes=[ksqb])
        for c in range(NCH):
            csl = slice(c * TT, (c + 1) * TT)
            K.op(K.blk.tensor, K.sPE,
                 lambda t, csl=csl: t.matmul(pN.ap[0:1, :], K.onesb_col[0:64, :],
                                             ksqb.ap[0:64, csl], start=True, stop=True),
                 reads=[ksqb], writes=[pN], extra=[K.init_tok])
            K.op(K.blk.vector, K.sDV,
                 lambda v: v.tensor_copy(rowb.ap[0:1, :], pN.ap[0:1, :]),
                 reads=[pN], writes=[rowb])
            K.op(K.blk.vector, K.sDV,
                 lambda v: v.reduce_max(rowb.ap[0:1, 0:1], rowb.ap[0:1, :], axis=AX.X),
                 reads=[rowb], writes=[rowb])
            if c == 0:
                K.op(K.blk.vector, K.sDV,
                     lambda v, h=h: v.tensor_copy(kmaxs.ap[0:1, h:h + 1],
                                                  rowb.ap[0:1, 0:1]),
                     reads=[rowb], writes=[kmaxs])
            else:
                K.op(K.blk.vector, K.sDV,
                     lambda v, h=h: v.tensor_max(kmaxs.ap[0:1, h:h + 1],
                                                 kmaxs.ap[0:1, h:h + 1],
                                                 rowb.ap[0:1, 0:1]),
                     reads=[rowb], writes=[kmaxs])

        for qt in range(NTT):
            jq = qt % 2
            qb = Qbuf[jq]
            qsl = slice(qt * TT, (qt + 1) * TT)
            K.op(K.blk.sync, K.dsem(f"aql{jq}"),
                 lambda sy, h=h, qsl=qsl, qb=qb: sy.dma_start(
                     out=qb.ap[0:64, :], in_=q_sp[b, h, :, qsl]),
                 writes=[qb])
            K.op(K.blk.scalar, K.sSC,
                 lambda sc, qb=qb: sc.activation(qsqb.ap[0:64, :], qb.ap[0:64, :],
                                                 AF.Square),
                 reads=[qb], writes=[qsqb])
            K.op(K.blk.tensor, K.sPE,
                 lambda t: t.matmul(pN.ap[0:1, :], K.onesb_col[0:64, :],
                                    qsqb.ap[0:64, :], start=True, stop=True),
                 reads=[qsqb], writes=[pN])
            K.op(K.blk.vector, K.sDV,
                 lambda v, h=h: v.tensor_scalar(brow.ap[0:1, :], pN.ap[0:1, :],
                                                kmaxs.ap[0:1, h:h + 1], mh2,
                                                OP.add, OP.mult),
                 reads=[pN, kmaxs], writes=[brow])
            K.op(K.blk.vector, K.sDV,
                 lambda v, qb=qb: v.tensor_copy(qb.ap[64:65, :], brow.ap[0:1, :]),
                 reads=[brow], writes=[qb])
            for kc in range(NKC):
                ps = pS[kc % 2]
                ssl = slice(kc * TT, (kc + 1) * TT)
                K.op(K.blk.tensor, K.sPE,
                     lambda t, kc=kc, ps=ps, kb=kb, qb=qb: t.matmul(
                         ps.ap, kb.ap[0:65, kc * 128:(kc + 1) * 128],
                         qb.ap[0:65, :], start=True, stop=True),
                     reads=[kb, qb, konesA[j]], writes=[ps])
                K.op(K.blk.scalar, K.sSC,
                     lambda sc, ssl=ssl, ps=ps: sc.activation(Sbuf.ap[:, ssl], ps.ap,
                                                              AF.Exp),
                     reads=[ps], writes=[Sbuf])
                K.op(K.blk.tensor, K.sPE,
                     lambda t, kc=kc, ssl=ssl, vb=vb: t.matmul(
                         pO.ap[0:65, :], vb.ap[:, kc * 65:(kc + 1) * 65],
                         Sbuf.ap[:, ssl], start=(kc == 0), stop=(kc == NKC - 1)),
                     reads=[vb, Sbuf, vonesA[j]], writes=[pO])
            K.op(K.blk.vector, K.sDV,
                 lambda v: v.reciprocal(recrow.ap[0:1, :], pO.ap[64:65, :]),
                 reads=[pO], writes=[recrow])
            K.op(K.blk.tensor, K.sPE,
                 lambda t: t.matmul(pR.ap[0:64, :], K.ones_rowf[0:1, 0:64],
                                    recrow.ap[0:1, :], start=True, stop=True),
                 reads=[recrow], writes=[pR])
            K.op(K.blk.scalar, K.sSC,
                 lambda sc: sc.copy(recB.ap[0:64, :], pR.ap[0:64, :]),
                 reads=[pR], writes=[recB])
            ost = Ost[jq]
            K.op(K.blk.vector, K.sDV,
                 lambda v, ost=ost: v.scalar_tensor_tensor(
                     ost.ap[0:64, :], pO.ap[0:64, :], 0.0, recB.ap[0:64, :],
                     OP.bypass, OP.mult),
                 reads=[pO, recB], writes=[ost])
            K.op(K.blk.sync, K.dsem(f"aos{jq}"),
                 lambda sy, h=h, qsl=qsl, ost=ost: sy.dma_start(
                     out=o_sp[b, h * DH:(h + 1) * DH, qsl], in_=ost.ap[0:64, :]),
                 reads=[ost])


def emit_attn_wo(K, b, o_sp, wo_dr, attn_sp):
    K.phase(f"awo{b}")
    Wob = K.pbuf("Wob", NCH * 1024, BF16)
    ot = [K.pbuf(f"ot{j}", NCH * TT, BF16) for j in range(2)]
    ast = [K.pbuf(f"ast{j}", TT) for j in range(2)]
    pW = [K.ppsum("pW0"), K.ppsum("pW1")]

    for hc in range(NCH):
        rs = slice(hc * 128, (hc + 1) * 128)
        cs = slice(hc * 1024, (hc + 1) * 1024)
        K.op(K.blk.sync, K.dsem("wob"),
             lambda sy, rs=rs, cs=cs: sy.dma_start(out=Wob.ap[:, cs], in_=wo_dr[rs, :]),
             nowaw=[Wob])
    for qt in range(NTT):
        p = qt % 2
        qsl = slice(qt * TT, (qt + 1) * TT)
        K.op(K.blk.sync, K.dsem(f"wol{p}"),
             lambda sy, p=p, qsl=qsl: sy.dma_start(
                 out=ot[p].ap.rearrange("p (hc t) -> p hc t", hc=NCH),
                 in_=o_sp[b].rearrange("(hc p) t -> p hc t", p=128)[:, :, qsl]),
             writes=[ot[p]])
        for jc in range(NCH):
            pw = pW[jc % 2]
            for hc in range(NCH):
                K.op(K.blk.tensor, K.sPE,
                     lambda t, pw=pw, hc=hc, jc=jc, p=p: t.matmul(
                         pw.ap, Wob.ap[:, hc * 1024 + jc * 128:hc * 1024 + (jc + 1) * 128],
                         ot[p].ap[:, hc * TT:(hc + 1) * TT],
                         start=(hc == 0), stop=(hc == NCH - 1)),
                     reads=[Wob, ot[p]], writes=[pw], extra=[K.init_tok])
            st = ast[jc % 2]
            K.op(K.blk.vector, K.sDV,
                 lambda v, pw=pw, st=st: v.tensor_copy(st.ap, pw.ap),
                 reads=[pw], writes=[st])
            K.op(K.blk.sync, K.dsem(f"asp{jc % 2}"),
                 lambda sy, st=st, jc=jc, qsl=qsl: sy.dma_start(
                     out=attn_sp[b, jc * 128:(jc + 1) * 128, qsl], in_=st.ap),
                 reads=[st])



def emit_ffn0(K, b, xn_sp, xT_sp, attn_sp, wg_dr, wu_dr, wd_dr, x1_sp):
    K.phase(f"ffn0{b}")
    TTG = 2
    NQ = FFN0 // 1024
    xnr = K.pbuf("xnr", NCH * N, BF16)
    Wg = [K.pbuf("WgS", NCH * 1024, BF16)] * 2
    Wu = [K.pbuf("WuS", NCH * 1024, BF16)] * 2
    Wd = [K.pbuf("WdS", 8 * 1024, BF16)] * 2
    H = K.pbuf("H", 8 * TTG * TT, BF16)
    sg_st = [K.pbuf(f"sg{j}", TT, BF16) for j in range(2)]
    y_acc = K.pbuf("yacc", TTG * NCH * TT)
    xc = [K.pbuf(f"xc{j}", TT) for j in range(2)]
    ac = [K.pbuf(f"ac{j}", TT) for j in range(2)]
    x1st = [K.pbuf(f"x1st{j}", TT) for j in range(2)]
    pG = K.ppsum("pG")
    pU = K.ppsum("pU")
    pY = [K.ppsum("pY0"), K.ppsum("pY1")]

    for ci in range(NCH):
        K.op(K.blk.sync, K.dsem(f"fxn{ci % 2}"),
             lambda sy, ci=ci: sy.dma_start(
                 out=xnr.ap[:, ci * N:(ci + 1) * N],
                 in_=xn_sp[b, ci * 128:(ci + 1) * 128, :]),
             nowaw=[xnr])

    for tg in range(NTT // TTG):
        tts = [tg * TTG + u for u in range(TTG)]
        first_y = [[True] * NCH for _ in range(TTG)]
        for e in range(NE):
            for q in range(NQ):
                jw = 0
                wg, wu = Wg[0], Wu[0]
                qcols = slice(q * 1024, (q + 1) * 1024)
                K.op(K.blk.sync, K.dsem("fwg"),
                     lambda sy, e=e, qcols=qcols, wg=wg: sy.dma_start(
                         out=wg.ap.rearrange("p (ci h) -> p ci h", ci=NCH),
                         in_=wg_dr[e, :, qcols].rearrange("(ci p) h -> p ci h", p=128)),
                     writes=[wg])
                K.op(K.blk.scalar, K.dsem("fwu"),
                     lambda sc, e=e, qcols=qcols, wu=wu: sc.dma_start(
                         out=wu.ap.rearrange("p (ci h) -> p ci h", ci=NCH),
                         in_=wu_dr[e, :, qcols].rearrange("(ci p) h -> p ci h", p=128)),
                     writes=[wu])
                for u, tt in enumerate(tts):
                    xsl = lambda ci, tt=tt: slice(ci * N + tt * TT, ci * N + tt * TT + TT)
                    for hb in range(8):
                        for ci in range(NCH):
                            for (pp, ww) in ((pG, wg), (pU, wu)):
                                K.op(K.blk.tensor, K.sPE,
                                     lambda t, pp=pp, ww=ww, ci=ci, hb=hb, tt=tt: t.matmul(
                                         pp.ap,
                                         ww.ap[:, ci * 1024 + hb * 128:ci * 1024 + (hb + 1) * 128],
                                         xnr.ap[:, xsl(ci)],
                                         start=(ci == 0), stop=(ci == NCH - 1)),
                                     reads=[ww, xnr], writes=[pp], extra=[K.init_tok])
                        sg = sg_st[hb % 2]
                        K.op(K.blk.scalar, K.sSC,
                             lambda sc, sg=sg: sc.activation(sg.ap, pG.ap, AF.Silu),
                             reads=[pG], writes=[sg])
                        hsl = slice(hb * TTG * TT + u * TT, hb * TTG * TT + (u + 1) * TT)
                        K.op(K.blk.vector, K.sDV,
                             lambda v, hsl=hsl, sg=sg: v.scalar_tensor_tensor(
                                 H.ap[:, hsl], pU.ap, 0.0, sg.ap, OP.bypass, OP.mult),
                             reads=[pU, sg], writes=[H])
                wdb = Wd[0]
                K.op(K.blk.sync, K.dsem("fwd"),
                     lambda sy, e=e, q=q, wdb=wdb: sy.dma_start(
                         out=wdb.ap.rearrange("p (hc d) -> p hc d", hc=8),
                         in_=wd_dr[e, q * 1024:(q + 1) * 1024, :].rearrange(
                             "(hc p) d -> p hc d", p=128)),
                     writes=[wdb])
                for jc in range(NCH):
                    for hc in range(8):
                        for u in range(TTG):
                            K.op(K.blk.tensor, K.sPE,
                                 lambda t, u=u, wdb=wdb, hc=hc, jc=jc: t.matmul(
                                     pY[u].ap,
                                     wdb.ap[:, hc * 1024 + jc * 128:
                                            hc * 1024 + (jc + 1) * 128],
                                     H.ap[:, hc * TTG * TT + u * TT:
                                          hc * TTG * TT + (u + 1) * TT],
                                     start=(hc == 0), stop=(hc == 7)),
                                 reads=[wdb, H], writes=[pY[u]])
                    for u in range(TTG):
                        ysl = slice((u * NCH + jc) * TT, (u * NCH + jc + 1) * TT)
                        py = pY[u]
                        if first_y[u][jc]:
                            first_y[u][jc] = False
                            K.op(K.blk.vector, K.sDV,
                                 lambda v, ysl=ysl, py=py: v.tensor_copy(
                                     y_acc.ap[:, ysl], py.ap),
                                 reads=[py], writes=[y_acc])
                        else:
                            K.op(K.blk.vector, K.sDV,
                                 lambda v, ysl=ysl, py=py: v.tensor_add(
                                     y_acc.ap[:, ysl], py.ap, y_acc.ap[:, ysl]),
                                 reads=[py, y_acc], writes=[y_acc])
        for u, tt in enumerate(tts):
            tsl = slice(tt * TT, (tt + 1) * TT)
            for jc in range(NCH):
                p2 = jc % 2
                jsl = slice(jc * 128, (jc + 1) * 128)
                ysl = slice((u * NCH + jc) * TT, (u * NCH + jc + 1) * TT)
                K.op(K.blk.sync, K.dsem(f"cx{p2}"),
                     lambda sy, jsl=jsl, tsl=tsl, p2=p2: sy.dma_start(
                         out=xc[p2].ap, in_=xT_sp[b, jsl, tsl]),
                     writes=[xc[p2]])
                K.op(K.blk.sync, K.dsem(f"ca{p2}"),
                     lambda sy, jsl=jsl, tsl=tsl, p2=p2: sy.dma_start(
                         out=ac[p2].ap, in_=attn_sp[b, jsl, tsl]),
                     writes=[ac[p2]])
                K.op(K.blk.vector, K.sDV,
                     lambda v, p2=p2, ysl=ysl: v.tensor_add(
                         x1st[p2].ap, xc[p2].ap, y_acc.ap[:, ysl]),
                     reads=[xc[p2], y_acc], writes=[x1st[p2]])
                K.op(K.blk.vector, K.sDV,
                     lambda v, p2=p2: v.tensor_add(
                         x1st[p2].ap, x1st[p2].ap, ac[p2].ap),
                     reads=[ac[p2], x1st[p2]], writes=[x1st[p2]])
                K.op(K.blk.sync, K.dsem(f"cs{p2}"),
                     lambda sy, jsl=jsl, tsl=tsl, p2=p2: sy.dma_start(
                         out=x1_sp[b, jsl, tsl], in_=x1st[p2].ap),
                     reads=[x1st[p2]])




def emit_ln1(K, b, x1_sp, xn1_sp):
    K.phase(f"ln1{b}")
    inv_c = K.const(1.0 / C)
    eps = K.const(EPS)
    mhalf = K.const(-0.5)
    xr = [K.pbuf(f"xr{j}", NCH * TT) for j in range(2)]
    sq = K.pbuf("sq", TT)
    mrow = K.pbuf("mrow", TT)
    vrow = K.pbuf("vrow", TT)
    rrow = K.pbuf("rrow", TT)
    st = [K.pbuf(f"st{j}", TT) for j in range(2)]
    xnst = [K.pbuf(f"xnst{j}", TT, BF16) for j in range(2)]
    pM = K.ppsum("pM")
    pV = K.ppsum("pV")
    pB = K.ppsum("pB")
    pR = K.ppsum("pR")

    for tt in range(NTT):
        p = tt % 2
        x = xr[p]
        tsl = slice(tt * TT, (tt + 1) * TT)
        K.op(K.blk.sync, K.dsem(f"l1x{p}"),
             lambda sy, x=x, tsl=tsl: sy.dma_start(
                 out=x.ap.rearrange("p (ci t) -> p ci t", ci=NCH),
                 in_=x1_sp[b].rearrange("(ci p) t -> p ci t", p=128)[:, :, tsl]),
             writes=[x])
        for ci in range(NCH):
            csl = slice(ci * TT, (ci + 1) * TT)
            K.op(K.blk.tensor, K.sPE,
                 lambda t, x=x, csl=csl, ci=ci: t.matmul(
                     pM.ap[0:1, :], K.ones_col[:], x.ap[:, csl],
                     start=(ci == 0), stop=(ci == NCH - 1)),
                 reads=[x], writes=[pM], extra=[K.init_tok])
            K.op(K.blk.scalar, K.sSC,
                 lambda sc, x=x, csl=csl: sc.activation(sq.ap, x.ap[:, csl], AF.Square),
                 reads=[x], writes=[sq])
            K.op(K.blk.tensor, K.sPE,
                 lambda t, ci=ci: t.matmul(
                     pV.ap[0:1, :], K.ones_col[:], sq.ap,
                     start=(ci == 0), stop=(ci == NCH - 1)),
                 reads=[sq], writes=[pV])
        K.op(K.blk.vector, K.sDV,
             lambda v: v.tensor_scalar(mrow.ap[0:1, :], pM.ap[0:1, :], inv_c, None,
                                       OP.mult),
             reads=[pM], writes=[mrow])
        K.op(K.blk.vector, K.sDV,
             lambda v: v.tensor_mul(vrow.ap[0:1, :], mrow.ap[0:1, :], mrow.ap[0:1, :]),
             reads=[mrow], writes=[vrow])
        K.op(K.blk.vector, K.sDV,
             lambda v: v.scalar_tensor_tensor(vrow.ap[0:1, :], pV.ap[0:1, :], inv_c,
                                              vrow.ap[0:1, :], OP.mult, OP.subtract),
             reads=[pV, vrow], writes=[vrow])
        K.op(K.blk.scalar, K.sSC,
             lambda sc: sc.activation(vrow.ap[0:1, :], vrow.ap[0:1, :], AF.Ln, bias=eps),
             reads=[vrow], writes=[vrow])
        K.op(K.blk.scalar, K.sSC,
             lambda sc: sc.activation(rrow.ap[0:1, :], vrow.ap[0:1, :], AF.Exp,
                                      scale=mhalf),
             reads=[vrow], writes=[rrow])
        K.op(K.blk.tensor, K.sPE,
             lambda t: t.matmul(pB.ap, K.ones_rowf[0:1, :], mrow.ap[0:1, :],
                                start=True, stop=True),
             reads=[mrow], writes=[pB])
        K.op(K.blk.tensor, K.sPE,
             lambda t: t.matmul(pR.ap, K.ones_rowf[0:1, :], rrow.ap[0:1, :],
                                start=True, stop=True),
             reads=[rrow], writes=[pR])
        for ci in range(NCH):
            csl = slice(ci * TT, (ci + 1) * TT)
            s2_ = st[ci % 2]
            K.op(K.blk.vector, K.sDV,
                 lambda v, x=x, csl=csl, s2_=s2_: v.tensor_sub(s2_.ap, x.ap[:, csl],
                                                              pB.ap),
                 reads=[x, pB], writes=[s2_])
            K.op(K.blk.vector, K.sDV,
                 lambda v, s2_=s2_, p=p: v.tensor_mul(xnst[ci % 2].ap, s2_.ap, pR.ap),
                 reads=[s2_, pR], writes=[xnst[ci % 2]])
            K.op(K.blk.sync, K.dsem(f"l1s{ci % 2}"),
                 lambda sy, ci=ci, tsl=tsl: sy.dma_start(
                     out=xn1_sp[b, ci * 128:(ci + 1) * 128, tsl],
                     in_=xnst[ci % 2].ap),
                 reads=[xnst[ci % 2]])


def emit_agent(K, b, xn1_sp, wqkv_dr, wconv_dr, av_dr):
    K.phase(f"ag{b}")
    agsc = K.const(AGENT_SCALE)
    xn1r = K.pbuf("xn1r", NCH * N, BF16)
    wT = K.pbuf("wT", NCH * 1024, BF16)
    vT = K.pbuf("vT", NCH * N, BF16)
    wstg = K.pbuf("wstg", 1024, BF16)
    qst = K.pbuf("qst", TT, BF16)
    w1cb = K.pbuf("w1cb", NCH, BF16)
    aab = K.pbuf("aab", NCH, BF16)
    agrow = K.pbuf("agrow", N)
    agb = K.pbuf("agb", NTT * TT)
    tst = K.pbuf("tst", TT)
    tcol = K.pbuf("tcol", NCH)
    t1r = K.pbuf("t1r", 1024)
    rr = K.pbuf("rr", 2)
    pQ = K.ppsum("pQ")
    pA = K.ppsum("pA")
    pT = K.ppsum("pT", BF16)

    for ci in range(NCH):
        K.op(K.blk.sync, K.dsem(f"agx{ci % 2}"),
             lambda sy, ci=ci: sy.dma_start(
                 out=xn1r.ap[:, ci * N:(ci + 1) * N],
                 in_=xn1_sp[b, ci * 128:(ci + 1) * 128, :]),
             nowaw=[xn1r])
    K.op(K.blk.gpsimd, K.dsem("agwc"),
         lambda g: g.dma_start(out=w1cb.ap,
                               in_=wconv_dr.rearrange("(cc p) -> p cc", p=128),
                               allow_slow_non_contiguous=True),
         writes=[w1cb])

    def load_wT(dlo):
        for dc in range(8):
            K.op(K.blk.gpsimd, K.dsem("agw"),
                 lambda g, dc=dc: g.dma_start(
                     out=wstg.ap, in_=wqkv_dr[dlo + dc * 128:dlo + (dc + 1) * 128, :]),
                 writes=[wstg])
            for ci in range(NCH):
                K.op(K.blk.tensor, K.sPE,
                     lambda t, ci=ci: t.transpose(
                         pT.ap[:, 0:128], wstg.ap[:, ci * 128:(ci + 1) * 128],
                         K.ident[:]),
                     reads=[wstg], writes=[pT], extra=[K.init_tok])
                K.op(K.blk.vector, K.sDV,
                     lambda v, ci=ci, dc=dc: v.tensor_copy(
                         wT.ap[:, ci * 1024 + dc * 128:ci * 1024 + (dc + 1) * 128],
                         pT.ap[:, 0:128]),
                     reads=[pT], writes=[wT])

    def dproj(dc, tt, reads_extra=()):
        for ci in range(NCH):
            K.op(K.blk.tensor, K.sPE,
                 lambda t, ci=ci, dc=dc, tt=tt: t.matmul(
                     pQ.ap, wT.ap[:, ci * 1024 + dc * 128:ci * 1024 + (dc + 1) * 128],
                     xn1r.ap[:, ci * N + tt * TT:ci * N + tt * TT + TT],
                     start=(ci == 0), stop=(ci == NCH - 1)),
                 reads=[wT, xn1r], writes=[pQ])

    load_wT(0)
    for tt in range(NTT):
        tsl = slice(tt * TT, (tt + 1) * TT)
        for dc in range(NCH):
            dproj(dc, tt)
            K.op(K.blk.vector, K.sDV, lambda v: v.tensor_copy(qst.ap, pQ.ap),
                 reads=[pQ], writes=[qst])
            K.op(K.blk.tensor, K.sPE,
                 lambda t, dc=dc: t.matmul(pA.ap[0:1, :], w1cb.ap[:, dc:dc + 1],
                                           qst.ap, start=(dc == 0),
                                           stop=(dc == NCH - 1)),
                 reads=[w1cb, qst], writes=[pA])
        K.op(K.blk.vector, K.sDV,
             lambda v, tsl=tsl: v.tensor_copy(agrow.ap[0:1, tsl], pA.ap[0:1, :]),
             reads=[pA], writes=[agrow])
    K.op(K.blk.vector, K.sDV,
         lambda v: v.tensor_scalar(agrow.ap[0:1, :], agrow.ap[0:1, :], agsc, None,
                                   OP.mult),
         reads=[agrow], writes=[agrow])

    def bcast_row(rowbuf):
        for tt in range(NTT):
            tsl = slice(tt * TT, (tt + 1) * TT)
            K.op(K.blk.tensor, K.sPE,
                 lambda t, tsl=tsl: t.matmul(pA.ap, K.ones_rowf[0:1, :],
                                             rowbuf.ap[0:1, tsl],
                                             start=True, stop=True),
                 reads=[rowbuf], writes=[pA])
            K.op(K.blk.scalar, K.sSC,
                 lambda sc, tsl=tsl: sc.copy(agb.ap[:, tsl], pA.ap),
                 reads=[pA], writes=[agb])
    bcast_row(agrow)

    def wsum_cols(getsrc, out_first):
        first = [True] * NCH
        for tt in range(NTT):
            tsl = slice(tt * TT, (tt + 1) * TT)
            for cc in range(NCH):
                srcbuf, srcap = getsrc(cc, tt)
                K.op(K.blk.vector, K.sDV,
                     lambda v, srcap=srcap, tsl=tsl: v.tensor_mul(
                         tst.ap, srcap, agb.ap[:, tsl]),
                     reads=[srcbuf, agb], writes=[tst])
                if first[cc]:
                    first[cc] = False
                    K.op(K.blk.vector, K.sDV,
                         lambda v, cc=cc: v.reduce_sum(tcol.ap[:, cc:cc + 1],
                                                       tst.ap, axis=AX.X),
                         reads=[tst], writes=[tcol])
                else:
                    K.op(K.blk.vector, K.sDV,
                         lambda v: v.reduce_sum(tst.ap[:, 0:1], tst.ap, axis=AX.X),
                         reads=[tst], writes=[tst])
                    K.op(K.blk.vector, K.sDV,
                         lambda v, cc=cc: v.tensor_add(tcol.ap[:, cc:cc + 1],
                                                       tcol.ap[:, cc:cc + 1],
                                                       tst.ap[:, 0:1]),
                         reads=[tst, tcol], writes=[tcol])

    load_wT(C)
    kbufs = {}
    def ksrc(cc, tt):
        dproj(cc, tt)
        K.op(K.blk.vector, K.sDV, lambda v: v.tensor_copy(qst.ap, pQ.ap),
             reads=[pQ], writes=[qst])
        return qst, qst.ap
    wsum_cols(ksrc, tcol)

    def row_softmax(rowbuf):
        K.op(K.blk.vector, K.sDV,
             lambda v: v.reduce_max(rr.ap[0:1, 0:1], rowbuf.ap[0:1, :], axis=AX.X,
                                    negate=True),
             reads=[rowbuf], writes=[rr])
        K.op(K.blk.scalar, K.sSC,
             lambda sc: sc.activation(rowbuf.ap[0:1, :], rowbuf.ap[0:1, :], AF.Exp,
                                      bias=rr.ap[0:1, 0:1],
                                      accum_out=rr.ap[0:1, 1:2]),
             reads=[rowbuf, rr], writes=[rowbuf, rr])
        K.op(K.blk.vector, K.sDV,
             lambda v: v.reciprocal(rr.ap[0:1, 1:2], rr.ap[0:1, 1:2]),
             reads=[rr], writes=[rr])
        K.op(K.blk.vector, K.sDV,
             lambda v: v.tensor_scalar(rowbuf.ap[0:1, :], rowbuf.ap[0:1, :],
                                       rr.ap[0:1, 1:2], None, OP.mult),
             reads=[rowbuf, rr], writes=[rowbuf])

    av1b = K.pbuf("av1b", 1)
    av2b = K.pbuf("av2b", 1)
    K.op(K.blk.sync, K.dsem("agt1"),
         lambda sy: sy.dma_start(out=av_dr[b, 1].rearrange("(p cc) -> p cc", p=128),
                                 in_=tcol.ap),
         reads=[tcol], writes=[av1b])
    K.op(K.blk.sync, K.dsem("agt1"),
         lambda sy: sy.dma_start(out=t1r.ap[0:1, :],
                                 in_=av_dr[b, 1].unsqueeze(0)),
         reads=[av1b], writes=[t1r])
    row_softmax(t1r)
    K.op(K.blk.sync, K.dsem("agt1"),
         lambda sy: sy.dma_start(out=av_dr[b, 2].unsqueeze(0),
                                 in_=t1r.ap[0:1, :]),
         reads=[t1r], writes=[av2b])
    K.op(K.blk.gpsimd, K.dsem("agt2"),
         lambda g: g.dma_start(out=aab.ap,
                               in_=av_dr[b, 2].rearrange("(p cc) -> p cc", p=128)),
         reads=[av2b], writes=[aab])

    load_wT(2 * C)
    for tt in range(NTT):
        for cc in range(NCH):
            dproj(cc, tt)
            K.op(K.blk.vector, K.sDV,
                 lambda v, cc=cc, tt=tt: v.tensor_copy(
                     vT.ap[:, cc * N + tt * TT:cc * N + tt * TT + TT], pQ.ap),
                 reads=[pQ], writes=[vT])
    for tt in range(NTT):
        tsl = slice(tt * TT, (tt + 1) * TT)
        for cc in range(NCH):
            K.op(K.blk.tensor, K.sPE,
                 lambda t, cc=cc, tt=tt: t.matmul(
                     pA.ap[0:1, :], aab.ap[:, cc:cc + 1],
                     vT.ap[:, cc * N + tt * TT:cc * N + tt * TT + TT],
                     start=(cc == 0), stop=(cc == NCH - 1)),
                 reads=[aab, vT], writes=[pA])
        K.op(K.blk.vector, K.sDV,
             lambda v, tsl=tsl: v.tensor_copy(agrow.ap[0:1, tsl], pA.ap[0:1, :]),
             reads=[pA], writes=[agrow])
    row_softmax(agrow)
    bcast_row(agrow)
    def vsrc(cc, tt):
        return vT, vT.ap[:, cc * N + tt * TT:cc * N + tt * TT + TT]
    wsum_cols(vsrc, tcol)
    K.op(K.blk.sync, K.dsem("agav"),
         lambda sy: sy.dma_start(out=av_dr[b, 0].rearrange("(p cc) -> p cc", p=128),
                                 in_=tcol.ap),
         reads=[tcol])


def emit_gate(K, b, xn1_sp, gw_dr, comb_dr):
    K.phase(f"gate{b}")
    one_ = K.const(1.0)
    xt = [K.pbuf(f"xt{j}", NCH * 128, BF16) for j in range(2)]
    gwb = K.pbuf("gwb", NCH * NE, BF16)
    lg = K.pbuf("lg", NE)
    gt = K.pbuf("gt", NE * NE)
    cnt = K.pbuf("cnt", NE)
    msk = K.pbuf("msk", NE)
    comb = K.pbuf("comb", 16 * NE)
    nmx = K.pbuf("nmx", 2)
    pL = K.ppsum("pL")
    pC = K.ppsum("pC2f")
    cst = K.pbuf("cst", 128)

    for ci in range(NCH):
        K.op(K.blk.gpsimd, K.dsem("ggw"),
             lambda g, ci=ci: g.dma_start(
                 out=gwb.ap[:, ci * NE:(ci + 1) * NE],
                 in_=gw_dr[:, ci * 128:(ci + 1) * 128].rearrange("e p -> p e"),
                 allow_slow_non_contiguous=True),
             nowaw=[gwb])

    for tk in range(N // 128):
        p = tk % 2
        x = xt[p]
        tsl = slice(tk * 128, (tk + 1) * 128)
        K.op(K.blk.sync, K.dsem(f"gx{p}"),
             lambda sy, x=x, tsl=tsl: sy.dma_start(
                 out=x.ap.rearrange("p (ci t) -> p ci t", ci=NCH),
                 in_=xn1_sp[b].rearrange("(ci p) t -> p ci t", p=128)[:, :, tsl]),
             writes=[x])
        for ci in range(NCH):
            K.op(K.blk.tensor, K.sPE,
                 lambda t, x=x, ci=ci: t.matmul(
                     pL.ap[:, 0:NE], x.ap[:, ci * 128:(ci + 1) * 128],
                     gwb.ap[:, ci * NE:(ci + 1) * NE],
                     start=(ci == 0), stop=(ci == NCH - 1)),
                 reads=[x, gwb], writes=[pL], extra=[K.init_tok])
        K.op(K.blk.vector, K.sDV,
             lambda v: v.reduce_max(nmx.ap[:, 0:1], pL.ap[:, 0:NE], axis=AX.X,
                                    negate=True),
             reads=[pL], writes=[nmx])
        K.op(K.blk.scalar, K.sSC,
             lambda sc: sc.activation(lg.ap, pL.ap[:, 0:NE], AF.Exp,
                                      bias=nmx.ap[:, 0:1],
                                      accum_out=nmx.ap[:, 1:2]),
             reads=[pL, nmx], writes=[lg, nmx])
        K.op(K.blk.vector, K.sDV,
             lambda v: v.reciprocal(nmx.ap[:, 1:2], nmx.ap[:, 1:2]),
             reads=[nmx], writes=[nmx])
        K.op(K.blk.vector, K.sDV,
             lambda v: v.tensor_scalar(lg.ap, lg.ap, nmx.ap[:, 1:2], None, OP.mult),
             reads=[lg, nmx], writes=[lg])
        in0 = lg.ap.unsqueeze(1).broadcast_to([128, NE, NE])
        in1 = lg.ap.unsqueeze(2).broadcast_to([128, NE, NE])
        K.op(K.blk.vector, K.sDV,
             lambda v: v.tensor_tensor(gt.ap.rearrange("p (a b) -> p a b", a=NE),
                                       in0, in1, OP.is_gt),
             reads=[lg], writes=[gt])
        K.op(K.blk.vector, K.sDV,
             lambda v: v.reduce_sum(cnt.ap.unsqueeze(2),
                                    gt.ap.rearrange("p (a b) -> p a b", a=NE),
                                    axis=AX.X),
             reads=[gt], writes=[cnt])
        K.op(K.blk.vector, K.sDV,
             lambda v: v.tensor_single_scalar(msk.ap, cnt.ap, one_, OP.is_le),
             reads=[cnt], writes=[msk])
        K.op(K.blk.vector, K.sDV,
             lambda v, tk=tk: v.tensor_mul(comb.ap[:, (tk % 16) * NE:
                                                   (tk % 16 + 1) * NE],
                                           lg.ap, msk.ap),
             reads=[lg, msk], writes=[comb])
        if tk % 16 == 15:
            g16 = tk // 16
            K.op(K.blk.tensor, K.sPE,
                 lambda t: t.transpose(pC.ap[:, 0:128], comb.ap, K.identf[:]),
                 reads=[comb], writes=[pC])
            K.op(K.blk.vector, K.sDV,
                 lambda v: v.tensor_copy(cst.ap, pC.ap[:, 0:128]),
                 reads=[pC], writes=[cst])
            K.op(K.blk.sync, K.dsem("gcs"),
                 lambda sy, g16=g16: sy.dma_start(
                     out=comb_dr[b, :, g16 * 2048:(g16 + 1) * 2048].rearrange(
                         "e (tk t) -> tk e t", t=128),
                     in_=cst.ap.unsqueeze(1)),
                 reads=[cst])


def emit_moe1(K, b, xn1_sp, x1_sp, av_dr, comb_dr,
              w1_dr, w2_dr, w3_dr, sw1_dr, sw2_dr, sw3_dr, out_dr):
    K.phase(f"moe1{b}")
    TTG = 2
    xnr = K.pbuf("xnr", NCH * N, BF16)
    Wg = [K.pbuf("WgS", NCH * 1024, BF16)] * 2
    Wu = [K.pbuf("WuS", NCH * 1024, BF16)] * 2
    Wd = [K.pbuf("WdS", 8 * 1024, BF16)] * 2
    H = K.pbuf("H", 8 * TTG * TT, BF16)
    sg_st = [K.pbuf(f"sg{j}", TT, BF16) for j in range(2)]
    y_acc = K.pbuf("yacc", TTG * NCH * TT)
    crow = K.pbuf("crow", TTG * TT)
    cbc = K.pbuf("cbc", TTG * TT)
    avc = K.pbuf("avc", NCH)
    xc = [K.pbuf("xc0", TT), K.pbuf("xc0", TT)]
    x1st = [K.pbuf(f"x1st{j}", TT) for j in range(2)]
    pG = K.ppsum("pG")
    pU = K.ppsum("pU")
    pY = [K.ppsum("pY0"), K.ppsum("pY1")]
    pB = K.ppsum("pBc")

    for ci in range(NCH):
        K.op(K.blk.sync, K.dsem(f"mxn{ci % 2}"),
             lambda sy, ci=ci: sy.dma_start(
                 out=xnr.ap[:, ci * N:(ci + 1) * N],
                 in_=xn1_sp[b, ci * 128:(ci + 1) * 128, :]),
             nowaw=[xnr])
    K.op(K.blk.sync, K.dsem("mav"),
         lambda sy: sy.dma_start(out=avc.ap,
                                 in_=av_dr[b, 0].rearrange("(p cc) -> p cc", p=128)),
         writes=[avc])

    SPECS = [(e, EXP_HID, [8, 8, 6]) for e in range(NE)] + [(8, SH_HID, [8, 4])]
    for tg in range(NTT // TTG):
        tts = [tg * TTG + u for u in range(TTG)]
        first_y = [[True] * NCH for _ in range(TTG)]
        for (e, HID, QHBS) in SPECS:
            shared = (e == NE)
            if not shared:
                K.op(K.blk.sync, K.dsem("mcr"),
                     lambda sy, e=e, tg=tg: sy.dma_start(
                         out=crow.ap[0:1, :],
                         in_=comb_dr[b, e, tg * TTG * TT:(tg + 1) * TTG * TT
                                     ].unsqueeze(0)),
                     writes=[crow])
                for u, tt in enumerate(tts):
                    tsl = slice(u * TT, (u + 1) * TT)
                    K.op(K.blk.tensor, K.sPE,
                         lambda t, tsl=tsl: t.matmul(pB.ap, K.ones_rowf[0:1, :],
                                                     crow.ap[0:1, tsl],
                                                     start=True, stop=True),
                         reads=[crow], writes=[pB], extra=[K.init_tok])
                    K.op(K.blk.scalar, K.sSC,
                         lambda sc, u=u: sc.copy(cbc.ap[:, u * TT:(u + 1) * TT],
                                                 pB.ap),
                         reads=[pB], writes=[cbc])
            hq0 = 0
            for qi, qhb in enumerate(QHBS):
                jw = 0
                wg, wu = Wg[0], Wu[0]
                qcols = slice(hq0 * 128, (hq0 + qhb) * 128)
                if shared:
                    g_src, u_src = sw1_dr[:, qcols], sw3_dr[:, qcols]
                else:
                    g_src, u_src = w1_dr[e, :, qcols], w3_dr[e, :, qcols]
                K.op(K.blk.sync, K.dsem("mwg"),
                     lambda g, g_src=g_src, qhb=qhb, wg=wg: g.dma_start(
                         out=wg.ap[:, 0:NCH * qhb * 128].rearrange(
                             "p (ci h) -> p ci h", ci=NCH),
                         in_=g_src.rearrange("(ci p) h -> p ci h", p=128)),
                     writes=[wg])
                K.op(K.blk.scalar, K.dsem("mwu"),
                     lambda g, u_src=u_src, qhb=qhb, wu=wu: g.dma_start(
                         out=wu.ap[:, 0:NCH * qhb * 128].rearrange(
                             "p (ci h) -> p ci h", ci=NCH),
                         in_=u_src.rearrange("(ci p) h -> p ci h", p=128)),
                     writes=[wu])
                for u, tt in enumerate(tts):
                    for hb in range(qhb):
                        for ci in range(NCH):
                            for (pp, ww) in ((pG, wg), (pU, wu)):
                                K.op(K.blk.tensor, K.sPE,
                                     lambda t, pp=pp, ww=ww, ci=ci, hb=hb, tt=tt,
                                     qhb=qhb: t.matmul(
                                         pp.ap,
                                         ww.ap[:, ci * qhb * 128 + hb * 128:
                                               ci * qhb * 128 + (hb + 1) * 128],
                                         xnr.ap[:, ci * N + tt * TT:
                                                ci * N + tt * TT + TT],
                                         start=(ci == 0), stop=(ci == NCH - 1)),
                                     reads=[ww, xnr], writes=[pp])
                        sg = sg_st[hb % 2]
                        K.op(K.blk.scalar, K.sSC,
                             lambda sc, sg=sg: sc.activation(sg.ap, pG.ap, AF.Silu),
                             reads=[pG], writes=[sg])
                        hsl = slice(hb * TTG * TT + u * TT,
                                    hb * TTG * TT + (u + 1) * TT)
                        K.op(K.blk.vector, K.sDV,
                             lambda v, hsl=hsl, sg=sg: v.scalar_tensor_tensor(
                                 H.ap[:, hsl], pU.ap, 0.0, sg.ap, OP.bypass,
                                 OP.mult),
                             reads=[pU, sg], writes=[H])
                    if not shared:
                        hap = H.ap.rearrange("p (hc u t) -> p hc u t", hc=8,
                                             u=TTG)[:, 0:qhb, u, :]
                        cap = cbc.ap[:, u * TT:(u + 1) * TT].unsqueeze(1)
                        K.op(K.blk.vector, K.sDV,
                             lambda v, hap=hap, cap=cap, qhb=qhb: v.tensor_tensor(
                                 hap, hap, cap.broadcast_to([128, qhb, TT]),
                                 OP.mult),
                             reads=[H, cbc], writes=[H])
                wdb = Wd[0]
                d_src = sw2_dr[qcols, :] if shared else w2_dr[e, qcols, :]
                K.op(K.blk.sync, K.dsem("mwd"),
                     lambda g, d_src=d_src, qhb=qhb, wdb=wdb: g.dma_start(
                         out=wdb.ap[:, 0:qhb * 1024].rearrange(
                             "p (hc d) -> p hc d", hc=qhb),
                         in_=d_src.rearrange("(hc p) d -> p hc d", p=128)),
                     writes=[wdb])
                for jc in range(NCH):
                    for hc in range(qhb):
                        for u in range(TTG):
                            K.op(K.blk.tensor, K.sPE,
                                 lambda t, u=u, wdb=wdb, hc=hc, qhb=qhb, jc=jc:
                                 t.matmul(
                                     pY[u].ap,
                                     wdb.ap[:, hc * 1024 + jc * 128:
                                            hc * 1024 + (jc + 1) * 128],
                                     H.ap[:, hc * TTG * TT + u * TT:
                                          hc * TTG * TT + (u + 1) * TT],
                                     start=(hc == 0), stop=(hc == qhb - 1)),
                                 reads=[wdb, H], writes=[pY[u]])
                    for u, tt in enumerate(tts):
                        py = pY[u]
                        ysl = slice((u * NCH + jc) * TT, (u * NCH + jc + 1) * TT)
                        if first_y[u][jc]:
                            first_y[u][jc] = False
                            K.op(K.blk.vector, K.sDV,
                                 lambda v, ysl=ysl, py=py: v.tensor_copy(
                                     y_acc.ap[:, ysl], py.ap),
                                 reads=[py], writes=[y_acc])
                        else:
                            K.op(K.blk.vector, K.sDV,
                                 lambda v, ysl=ysl, py=py: v.tensor_add(
                                     y_acc.ap[:, ysl], py.ap, y_acc.ap[:, ysl]),
                                 reads=[py, y_acc], writes=[y_acc])
                hq0 += qhb
        for u, tt in enumerate(tts):
            tsl = slice(tt * TT, (tt + 1) * TT)
            for jc in range(NCH):
                p2 = jc % 2
                jsl = slice(jc * 128, (jc + 1) * 128)
                ysl = slice((u * NCH + jc) * TT, (u * NCH + jc + 1) * TT)
                K.op(K.blk.sync, K.dsem(f"mx{p2}"),
                     lambda sy, jsl=jsl, tsl=tsl, p2=p2: sy.dma_start(
                         out=xc[p2].ap, in_=x1_sp[b, jsl, tsl]),
                     writes=[xc[p2]])
                K.op(K.blk.vector, K.sDV,
                     lambda v, p2=p2, ysl=ysl: v.tensor_add(
                         x1st[p2].ap, xc[p2].ap, y_acc.ap[:, ysl]),
                     reads=[xc[p2], y_acc], writes=[x1st[p2]])
                K.op(K.blk.vector, K.sDV,
                     lambda v, p2=p2, jc=jc: v.tensor_scalar(
                         x1st[p2].ap, x1st[p2].ap, avc.ap[:, jc:jc + 1], None,
                         OP.add),
                     reads=[avc, x1st[p2]], writes=[x1st[p2]])
                K.op(K.blk.sync, K.dsem(f"ms{p2}"),
                     lambda sy, jsl=jsl, tsl=tsl, p2=p2: sy.dma_start(
                         out=out_dr[b, jsl, tsl], in_=x1st[p2].ap),
                     reads=[x1st[p2]])


def emit_exit(K, b, x2_sp, out_dr):
    K.phase(f"ex{b}")
    xin = [K.pbuf(f"xin{j}", NCH * 128) for j in range(2)]
    xout = [K.pbuf(f"xout{j}", NCH * 128) for j in range(2)]
    pX = [K.ppsum("pX0"), K.ppsum("pX1")]
    for tk in range(N // 128):
        p = tk % 2
        x = xin[p]
        tsl = slice(tk * 128, (tk + 1) * 128)
        K.op(K.blk.sync, K.dsem(f"exl{p}"),
             lambda sy, x=x, tsl=tsl: sy.dma_start(
                 out=x.ap.rearrange("p (ci t) -> p ci t", ci=NCH),
                 in_=x2_sp[b].rearrange("(ci p) t -> p ci t", p=128)[:, :, tsl]),
             writes=[x])
        for ci in range(NCH):
            px = pX[ci % 2]
            K.op(K.blk.tensor, K.sPE,
                 lambda t, x=x, ci=ci, px=px: t.transpose(
                     px.ap[:, 0:128], x.ap[:, ci * 128:(ci + 1) * 128], K.identf[:]),
                 reads=[x], writes=[px], extra=[K.init_tok])
            K.op(K.blk.vector, K.sDV,
                 lambda v, ci=ci, px=px, p=p: v.tensor_copy(
                     xout[p].ap[:, ci * 128:(ci + 1) * 128], px.ap[:, 0:128]),
                 reads=[px], writes=[xout[p]])
        K.op(K.blk.sync, K.dsem(f"exs{p}"),
             lambda sy, p=p, tk=tk: sy.dma_start(
                 out=out_dr[(b * N + tk * 128):(b * N + (tk + 1) * 128), :],
                 in_=xout[p].ap),
             reads=[xout[p]])


def build_program(phases=("all",), batches=range(BPC)):
    nc = bass.Bass("TRN2", target_bir_lowering=False, debug=False)
    dr = {}
    def dram(name, shape, dtype=F32, kind="Internal"):
        dr[name] = nc.dram_tensor(name, shape, dtype, kind=kind).ap()
        return dr[name]

    x_in = dram("x", [BPC * N, C], kind="ExternalInput")
    g0 = dram("L0_attn_ln_gamma", [C], kind="ExternalInput")
    b0 = dram("L0_attn_ln_beta", [C], kind="ExternalInput")
    wq_dr = dram("L0_attn_Wq", [C, C], kind="ExternalInput")
    wk_dr = dram("L0_attn_Wk", [C, C], kind="ExternalInput")
    wv_dr = dram("L0_attn_Wv", [C, C], kind="ExternalInput")
    xT_sp = dram("xT_sp", [BPC, C, N])
    xn_sp = dram("xn_sp", [BPC, C, N], BF16)
    xln_sp = dram("xln_sp", [BPC, C, N], BF16)
    q_sp = dram("q_sp", [BPC, HEADS, DH, N], BF16)
    k_sp = dram("k_sp", [BPC, HEADS, DH, N], BF16)
    v_sp = dram("v_sp", [BPC, N, C], BF16)
    wo_dr = dram("L0_attn_Wo", [C, C], kind="ExternalInput")
    wg_dr = dram("L0_ffn_Wg", [NE, C, FFN0], kind="ExternalInput")
    wu_dr = dram("L0_ffn_Wu", [NE, C, FFN0], kind="ExternalInput")
    wd_dr = dram("L0_ffn_Wd", [NE, FFN0, C], kind="ExternalInput")
    x1_sp = dram("x1_sp", [BPC, C, N])
    wqkv_dr = dram("L1_attn_w_qkv", [3 * C, C], kind="ExternalInput")
    wconv_dr = dram("L1_attn_w_conv1", [C], kind="ExternalInput")
    gw_dr = dram("L1_ffn_gate_w", [NE, C], kind="ExternalInput")
    w1_dr = dram("L1_ffn_w1", [NE, C, EXP_HID], kind="ExternalInput")
    w2_dr = dram("L1_ffn_w2", [NE, EXP_HID, C], kind="ExternalInput")
    w3_dr = dram("L1_ffn_w3", [NE, C, EXP_HID], kind="ExternalInput")
    sw1_dr = dram("L1_ffn_sw1", [C, SH_HID], kind="ExternalInput")
    sw2_dr = dram("L1_ffn_sw2", [SH_HID, C], kind="ExternalInput")
    sw3_dr = dram("L1_ffn_sw3", [C, SH_HID], kind="ExternalInput")
    xn1_sp = dram("xn1_sp", [BPC, C, N], BF16)
    av_dr = dram("av_dr", [BPC, 3, C])
    comb_dr = dram("comb_dr", [BPC, NE, N])
    x2_sp = dram("x2_sp", [BPC, C, N])
    out_dr = dram("out", [BPC * N, C], kind="ExternalOutput")
    o_sp = dram("o_sp", [BPC, C, N], BF16)
    attn_sp = dram("attn_sp", [BPC, C, N])

    wqB = dram("wqB", [C, C], BF16)
    wkB = dram("wkB", [C, C], BF16)
    wvB = dram("wvB", [C, C], BF16)
    woB = dram("woB", [C, C], BF16)
    wgB = dram("wgB", [NE, C, FFN0], BF16)
    wuB = dram("wuB", [NE, C, FFN0], BF16)
    wdB = dram("wdB", [NE, FFN0, C], BF16)
    wqkvB = dram("wqkvB", [3 * C, C], BF16)
    w1B = dram("w1B", [NE, C, EXP_HID], BF16)
    w2B = dram("w2B", [NE, EXP_HID, C], BF16)
    w3B = dram("w3B", [NE, C, EXP_HID], BF16)
    sw1B = dram("sw1B", [C, SH_HID], BF16)
    sw2B = dram("sw2B", [SH_HID, C], BF16)
    sw3B = dram("sw3B", [C, SH_HID], BF16)
    casts = [(wq_dr[:], wqB[:]), (wk_dr[:], wkB[:]), (wv_dr[:], wvB[:]),
             (wo_dr[:], woB[:]), (wqkv_dr[:], wqkvB[:]),
             (sw1_dr[:], sw1B[:]), (sw2_dr[:], sw2B[:]), (sw3_dr[:], sw3B[:])]
    for e in range(NE):
        casts += [(wg_dr[e], wgB[e]), (wu_dr[e], wuB[e]), (wd_dr[e], wdB[e]),
                  (w1_dr[e], w1B[e]), (w2_dr[e], w2B[e]), (w3_dr[e], w3B[e])]

    stack = contextlib.ExitStack()
    with stack:
        K = Kit(nc, stack)
        K.const(EPS)
        K.const(-0.5)
        K.const(ATT_SCALE1)
        K.const(1.0 / C)
        K.const(AGENT_SCALE)
        emit_init(K, g0, b0)
        if "all" in phases or len(phases) > 1 or "wcast" in phases:
            emit_wcast(K, casts)
        for b in batches:
            if "entry" in phases or "all" in phases:
                emit_entry(K, b, x_in, xT_sp, xn_sp, xln_sp)
            if "qkv" in phases or "all" in phases:
                emit_qkv(K, b, xln_sp, wqB, wkB, wvB, q_sp, k_sp, v_sp)
            if "attn" in phases or "all" in phases:
                emit_attn(K, b, q_sp, k_sp, v_sp, wo_dr, o_sp, attn_sp)
                emit_attn_wo(K, b, o_sp, woB, attn_sp)
            if "ffn0" in phases or "all" in phases:
                emit_ffn0(K, b, xn_sp, xT_sp, attn_sp, wgB, wuB, wdB, x1_sp)
            if "ln1" in phases or "all" in phases:
                emit_ln1(K, b, x1_sp, xn1_sp)
            if "agent" in phases or "all" in phases:
                emit_agent(K, b, xn1_sp, wqkvB, wconv_dr, av_dr)
            if "gate" in phases or "all" in phases:
                emit_gate(K, b, xn1_sp, gw_dr, comb_dr)
            if "moe1" in phases or "all" in phases:
                emit_moe1(K, b, xn1_sp, x1_sp, av_dr, comb_dr,
                          w1B, w2B, w3B, sw1B, sw2B, sw3B, x2_sp)
            if "exit" in phases or "all" in phases:
                emit_exit(K, b, x2_sp, out_dr)
        K.flush()
        def fin(sy):
            sy.wait_ge(K.sSP.h, K.sSP.n)
            sy.wait_ge(K.sGD.h, K.sGD.n)
            for s in K._dsems.values():
                sy.wait_ge(s.h, s.n)
        K.blk.sync(fin)
    nc.finalize()
    return nc, dr


_WNAMES = ["L0_attn_ln_gamma", "L0_attn_ln_beta", "L0_attn_Wq", "L0_attn_Wk",
           "L0_attn_Wv", "L0_attn_Wo", "L0_ffn_Wg", "L0_ffn_Wu", "L0_ffn_Wd",
           "L1_attn_w_qkv", "L1_attn_w_conv1", "L1_ffn_gate_w", "L1_ffn_w1",
           "L1_ffn_w2", "L1_ffn_w3", "L1_ffn_sw1", "L1_ffn_sw2", "L1_ffn_sw3"]


def kernel(**inputs):
    x = np.ascontiguousarray(np.asarray(inputs["L0_attn_latents"], np.float32))
    x = x.reshape(B * N, C)
    weights = {n: np.ascontiguousarray(np.asarray(inputs[n], np.float32))
               for n in _WNAMES}
    nc, dr = build_program(phases=("all",))
    in_maps = []
    for c in range(NCORES):
        m = {"x": x[c * BPC * N:(c + 1) * BPC * N]}
        m.update(weights)
        in_maps.append(m)
    res = run_bass_kernel_spmd(nc, in_maps, core_ids=list(range(NCORES)))
    out = np.concatenate([r["out"] for r in res.results], axis=0)
    return out.reshape(B, N, C)
